# version 02 (final)
# baseline (speedup 1.0000x reference)
import numpy as np
import ml_dtypes
from contextlib import ExitStack
import concourse.bass as bass
import concourse.mybir as mybir
from concourse.bass_utils import run_bass_kernel_spmd

F32 = mybir.dt.float32
BF16 = mybir.dt.bfloat16
I32 = mybir.dt.int32
AF = mybir.ActivationFunctionType
ALU = mybir.AluOpType
AX = mybir.AxisListType

HID = 4096; PROJ = 18560; INTER = 8192
H = 128; PD = 64; NS = 128; G = 8; CS = 256; TOK = 1024
NCH = TOK // CS
EPS = 1e-5
NEG = -1.0e9
QCOUNT = 3484.0

DEBUG = False
PHASE_LIMIT = 99


def _ap3(base, dims):
    return bass.AP(base.tensor, base.offset, [list(base.ap[0])] + [list(d) for d in dims])


def build(debug=False, detect_races=True):
    nc = bass.Bass("TRN2", target_bir_lowering=False, detect_race_conditions=detect_races)
    xT = nc.dram_tensor("xT", [HID, TOK + 3], F32, kind="ExternalInput").ap()
    WpT = nc.dram_tensor("WpT", [HID, PROJ], F32, kind="ExternalInput").ap()
    WoT = nc.dram_tensor("WoT", [INTER, HID], F32, kind="ExternalInput").ap()
    WfT = nc.dram_tensor("WfT", [HID, HID], F32, kind="ExternalInput").ap()
    ATd = nc.dram_tensor("ATd", [HID, 128], F32, kind="ExternalInput").ap()
    BTd = nc.dram_tensor("BTd", [128, HID], F32, kind="ExternalInput").ap()
    convw = nc.dram_tensor("convw", [128, 320], F32, kind="ExternalInput").ap()
    convb = nc.dram_tensor("convb", [128, 80], F32, kind="ExternalInput").ap()
    dtbd = nc.dram_tensor("dtbd", [128, 1], F32, kind="ExternalInput").ap()
    alogd = nc.dram_tensor("alogd", [128, 1], F32, kind="ExternalInput").ap()
    dexpd = nc.dram_tensor("dexpd", [128, 64], F32, kind="ExternalInput").ap()
    nwexpd = nc.dram_tensor("nwexpd", [128, 64], F32, kind="ExternalInput").ap()
    ffnbd = nc.dram_tensor("ffnbd", [128, 32], F32, kind="ExternalInput").ap()
    rWTd = nc.dram_tensor("rWTd", [HID, 128], F32, kind="ExternalInput").ap()
    rbd = nc.dram_tensor("rbd", [1, 128], F32, kind="ExternalInput").ap()
    wmaskd = nc.dram_tensor("wmaskd", [1, TOK], F32, kind="ExternalInput").ap()
    cmskd = nc.dram_tensor("cmskd", [128, 3], F32, kind="ExternalInput").ap()
    outT = nc.dram_tensor("outT", [HID, TOK], F32, kind="ExternalOutput").ap()
    sp_hs = nc.dram_tensor("sp_hs", [64, 128, TOK], BF16).ap()
    sp_gate = nc.dram_tensor("sp_gate", [64, 128, TOK], BF16).ap()
    sp_xT = nc.dram_tensor("sp_xT", [NCH, 2, 128, INTER], BF16).ap()
    sp_g = nc.dram_tensor("sp_g", [NCH, 64, 128, CS], BF16).ap()
    sp_att = nc.dram_tensor("sp_att", [32, 128, TOK], BF16).ap()
    sp_xmid = nc.dram_tensor("sp_xmid", [32, 128, TOK], F32).ap()
    sp_ffn = nc.dram_tensor("sp_ffn", [32, 128, TOK], BF16).ap()
    cumd = nc.dram_tensor("cumd", [NCH, 128, CS], F32).ap()
    cst_in_t = nc.dram_tensor("cst_in", [128, INTER + 128], F32)
    cst_out_t = nc.dram_tensor("cst_out", [512, INTER + 128], F32)
    cq_in_t = nc.dram_tensor("cq_in", [128, 32], F32)
    cq_out_t = nc.dram_tensor("cq_out", [128, 32], F32)
    rt_d = nc.dram_tensor("rt_d", [1, 8], F32).ap()
    cst_in = cst_in_t.ap(); cst_out = cst_out_t.ap()
    cq_in = cq_in_t.ap(); cq_out = cq_out_t.ap()
    dbg = {}
    if debug:
        def dout(name, shape):
            dbg[name] = nc.dram_tensor("dbg_" + name, shape, F32, kind="ExternalOutput").ap()
        dout("xln", [32, 128, TOK + 3])
        dout("xbc", [16, 128, TOK])
        dout("dtc", [128, TOK])
        dout("cum", [128, NCH, CS])
        dout("T", [128, INTER])
        dout("Sent", [128, INTER])
        dout("g", [NCH, 64, 128, CS])
        dout("att", [32, 128, TOK])
        dout("xmid", [32, 128, TOK])
        dout("q", [128, 32])
        dout("rt", [1, 8])
        dout("rwide", [128, 128])
        dout("rmx", [128, 2])
        dout("lgr", [1, 128])
        dout("r3a", [2, 1, 1027])
        dout("r3b", [2, 1, 1027])
        dout("r4", [2, 1, 1027])
        dout("ffn", [32, 128, TOK])

    ops = {k: [] for k in "tvsgy"}

    def OP(e, fn):
        ops[e].append(fn)

    class Ctr:
        __slots__ = ("h", "v")

    root = ExitStack()

    def sem(name):
        c = Ctr()
        c.h = root.enter_context(nc.semaphore(name))
        c.v = 0
        return c

    bar = {e: sem("bar_" + e) for e in "tvsgy"}
    wp = [sem("wp0"), sem("wp1")]
    ldy = sem("ldy"); ldg = sem("ldg")
    pe = sem("pe"); ve = sem("ve"); se = sem("se")
    sp = [sem(f"sp{i}") for i in range(4)]
    rd = [sem(f"rd{i}") for i in range(4)]
    sxt = sem("sxt"); sbc = sem("sbc"); scc = sem("scc")
    gw = [sem("gw0"), sem("gw1")]
    sio = sem("sio")
    xw = [sem("xw0"), sem("xw1")]

    def W(e, c, v=None):
        val = c.v if v is None else v
        if val > 0:
            ops[e].append(lambda eng, h=c.h, vv=val: eng.wait_ge(h, vv))

    def INC(e, fn, c, n=1):
        c.v += n
        ops[e].append(lambda eng, f=fn, h=c.h, nn=n: f(eng).then_inc(h, nn))

    def barrier():
        for c in [ldy, sxt, sbc] + sp + rd + gw + xw:
            W("y", c)
        for c in [ldg, scc, sio] + wp:
            W("g", c)
        for e in "tvsgy":
            c = bar[e]
            c.v += 1
            if e == "t" or e == "y":
                ops[e].append(lambda eng, h=c.h: eng.nop().then_inc(h, 1))
            elif e == "s":
                ops[e].append(lambda eng, h=c.h: eng.copy(out=sdum[0:1, 0:1], in_=sdum[0:1, 0:1]).then_inc(h, 1))
            else:
                ops[e].append(lambda eng, h=c.h: eng.engine_nop().then_inc(h, 1))
        for e in "tvsgy":
            for o in "tvsgy":
                if o != e:
                    W(e, bar[o])

    def sbuf(stack, name, shape, dt, side="left"):
        return stack.enter_context(nc.sbuf_tensor(name, shape, dt, side=side))

    pf = [root.enter_context(nc.psum_tensor(f"pf{i}", [128, 512], F32)) for i in range(6)]
    pb = [root.enter_context(nc.psum_tensor(f"pb{i}", [128, 1024], BF16)) for i in range(2)]

    ones1f = sbuf(root, "ones1f", [1, 128], F32)
    sdum = sbuf(root, "sdum", [1, 1], F32)
    onescf = sbuf(root, "onescf", [128, 1], F32)
    onescb = sbuf(root, "onescb", [128, 1], BF16)
    idb = sbuf(root, "idb", [128, 128], BF16)
    idf = sbuf(root, "idf", [128, 128], F32)
    triu = sbuf(root, "triu", [128, 128], F32)
    mtri = sbuf(root, "mtri", [128, 128], F32)
    ioti = sbuf(root, "ioti", [128, 128], I32)
    convw_s = sbuf(root, "convw_s", [128, 320], F32)
    convb_s = sbuf(root, "convb_s", [128, 80], F32)
    dtb_s = sbuf(root, "dtb_s", [128, 1], F32)
    alog_s = sbuf(root, "alog_s", [128, 1], F32)
    dexp_s = sbuf(root, "dexp_s", [128, 64], F32)
    nwexp_s = sbuf(root, "nwexp_s", [128, 64], F32)
    ffnb_s = sbuf(root, "ffnb_s", [128, 32], F32)
    rb_s = sbuf(root, "rb_s", [1, 128], F32)
    cmsk_s = sbuf(root, "cmsk_s", [128, 3], F32)
    a_col = sbuf(root, "a_col", [128, 1], F32)
    rowA = sbuf(root, "rowA", [1, 1027], F32, side="right")
    rowB = sbuf(root, "rowB", [1, 1027], F32, side="right")
    rowC = sbuf(root, "rowC", [1, 1027], F32, side="right")
    rowD = sbuf(root, "rowD", [1, 1027], F32, side="right")
    BCr = sbuf(root, "BCr", [128, 16, TOK], BF16)
    dtres = sbuf(root, "dtres", [128, TOK], F32)

    INC("g", lambda g: g.iota(ioti[:, :], pattern=[[1, 128]], base=0, channel_multiplier=-1), sio, 1)
    for (dst, src) in [(convw_s, convw), (convb_s, convb), (dtb_s, dtbd), (alog_s, alogd),
                       (dexp_s, dexpd), (nwexp_s, nwexpd), (ffnb_s, ffnbd), (rb_s, rbd),
                       (cmsk_s, cmskd)]:
        INC("g", lambda g, d=dst, s=src: g.dma_start(out=d[:, :], in_=s), ldg, 16)
    W("v", ldg)
    W("v", sio)
    OP("v", lambda v: v.memset(ones1f[:, :], 1.0))
    OP("v", lambda v: v.memset(sdum[:, :], 0.0))
    OP("v", lambda v: v.memset(onescf[:, :], 1.0))
    OP("v", lambda v: v.memset(onescb[:, :], 1.0))
    OP("v", lambda v: v.tensor_scalar(out=idb[:, :], in0=ioti[:, :], scalar1=0, scalar2=None, op0=ALU.is_equal))
    OP("v", lambda v: v.tensor_scalar(out=idf[:, :], in0=ioti[:, :], scalar1=0, scalar2=None, op0=ALU.is_equal))
    OP("v", lambda v: v.tensor_scalar(out=triu[:, :], in0=ioti[:, :], scalar1=0, scalar2=None, op0=ALU.is_ge))
    OP("v", lambda v: v.tensor_scalar(out=mtri[:, :], in0=ioti[:, :], scalar1=0, scalar2=NEG, op0=ALU.is_lt, op1=ALU.mult))
    INC("v", lambda v: v.engine_nop(), ve, 1)
    W("s", ldg)
    INC("s", lambda s: s.activation(out=a_col[:, :], in_=alog_s[:, :], func=AF.Exp), se, 1)
    aexp_se = se.v
    init_ve = ve.v
    W("t", ve, init_ve)

    p0 = ExitStack()
    xlnS = ExitStack()
    xln = sbuf(xlnS, "xln", [128, 32, TOK + 3], BF16, side="right")
    xld = [sbuf(p0, f"xld{i}", [128, TOK + 3], F32) for i in range(2)]
    sqst = [sbuf(p0, f"sqst{i}", [128, TOK + 3], F32) for i in range(2)]
    tmpx = sbuf(p0, "tmpx", [128, 1027], F32)
    xT3 = xT.rearrange("(a p) t -> p a t", p=128)

    rng3 = [(0, 512, 0, 2), (512, 1024, 1, 3), (899, 1027, 4, 5)]
    for i in range(32):
        sl = i % 2
        if i >= 2:
            W("y", pe, pe.v - 1)
        INC("y", lambda y, ii=i, s=sl: y.dma_start(out=xld[s][:, :], in_=xT3[:, ii, :]), rd[sl], 16)
        W("s", rd[sl])
        if i >= 2:
            W("s", pe, pe.v - 1)
        INC("s", lambda s_, s=sl: s_.activation(out=sqst[s][:, :], in_=xld[s][:, :], func=AF.Square), se, 1)
        W("t", rd[sl]); W("t", se)
        for ri, (c0, c1, pm, ps_) in enumerate(rng3):
            OP("t", lambda t, s=sl, a=c0, b=c1, p=pm, f=(i == 0), l=(i == 31): t.matmul(
                pf[p][0:1, 0:b - a], lhsT=onescf[:, :], rhs=xld[s][:, a:b], start=f, stop=l))
            fn_ = lambda t, s=sl, a=c0, b=c1, p=ps_, f=(i == 0), l=(i == 31): t.matmul(
                pf[p][0:1, 0:b - a], lhsT=onescf[:, :], rhs=sqst[s][:, a:b], start=f, stop=l)
            if ri == 2:
                INC("t", fn_, pe, 1)
            else:
                OP("t", fn_)
    W("v", pe)
    OP("v", lambda v: v.tensor_copy(out=rowA[:, 0:512], in_=pf[0][0:1, 0:512]))
    OP("v", lambda v: v.tensor_copy(out=rowA[:, 512:1024], in_=pf[1][0:1, 0:512]))
    OP("v", lambda v: v.tensor_copy(out=rowA[:, 1024:1027], in_=pf[4][0:1, 125:128]))
    OP("v", lambda v: v.tensor_copy(out=rowC[:, 0:512], in_=pf[2][0:1, 0:512]))
    OP("v", lambda v: v.tensor_copy(out=rowC[:, 512:1024], in_=pf[3][0:1, 0:512]))
    OP("v", lambda v: v.tensor_copy(out=rowC[:, 1024:1027], in_=pf[5][0:1, 125:128]))
    INC("v", lambda v: v.engine_nop(), ve, 1)

    def stats_finalize(n_feat, width):
        W("v", ve)
        OP("v", lambda v: v.tensor_scalar(out=rowA[:, 0:width], in0=rowA[:, 0:width],
                                          scalar1=1.0 / n_feat, scalar2=None, op0=ALU.mult))
        OP("v", lambda v: v.tensor_tensor(out=rowB[:, 0:width], in0=rowA[:, 0:width],
                                          in1=rowA[:, 0:width], op=ALU.mult))
        OP("v", lambda v: v.scalar_tensor_tensor(out=rowD[:, 0:width], in0=rowC[:, 0:width],
                                                 scalar=1.0 / n_feat, in1=rowB[:, 0:width],
                                                 op0=ALU.mult, op1=ALU.subtract))
        INC("v", lambda v: v.tensor_scalar(out=rowD[:, 0:width], in0=rowD[:, 0:width],
                                           scalar1=EPS, scalar2=None, op0=ALU.add), ve, 1)
        W("s", ve)
        OP("s", lambda s: s.activation(out=rowB[:, 0:width], in_=rowD[:, 0:width], func=AF.Ln))
        INC("s", lambda s: s.activation(out=rowB[:, 0:width], in_=rowB[:, 0:width],
                                        func=AF.Exp, scale=-0.5), se, 1)
        W("v", se)
        OP("v", lambda v: v.tensor_tensor(out=rowC[:, 0:width], in0=rowB[:, 0:width],
                                          in1=rowB[:, 0:width], op=ALU.mult))
        OP("v", lambda v: v.scalar_tensor_tensor(out=rowC[:, 0:width], in0=rowC[:, 0:width],
                                                 scalar=-0.5, in1=rowD[:, 0:width],
                                                 op0=ALU.mult, op1=ALU.mult))
        OP("v", lambda v: v.tensor_scalar(out=rowC[:, 0:width], in0=rowC[:, 0:width],
                                          scalar1=1.5, scalar2=None, op0=ALU.add))
        INC("v", lambda v: v.tensor_tensor(out=rowB[:, 0:width], in0=rowB[:, 0:width],
                                           in1=rowC[:, 0:width], op=ALU.mult), ve, 1)

    stats_finalize(HID, 1027)
    W("t", ve)
    for wi, (row, p0_, p1_, p2_) in enumerate([(rowA, 0, 1, 4), (rowB, 2, 3, 5)]):
        OP("t", lambda t, r=row, p=p0_: t.matmul(pf[p][:, 0:512], lhsT=ones1f[:, :], rhs=r[:, 0:512], start=True, stop=True))
        OP("t", lambda t, r=row, p=p1_: t.matmul(pf[p][:, 0:512], lhsT=ones1f[:, :], rhs=r[:, 512:1024], start=True, stop=True))
        fn_ = lambda t, r=row, p=p2_: t.matmul(pf[p][:, 0:128], lhsT=ones1f[:, :], rhs=r[:, 899:1027], start=True, stop=True)
        if wi == 1:
            INC("t", fn_, pe, 1)
        else:
            OP("t", fn_)
    bcast_pe = pe.v
    for i in range(32):
        sl = i % 2
        if i >= 2:
            W("y", ve, ve.v - 1)
        else:
            W("y", pe, bcast_pe)
        INC("y", lambda y, ii=i, s=sl: y.dma_start(out=xld[s][:, :], in_=xT3[:, ii, :]), rd[sl], 16)
        W("v", rd[sl]); W("v", pe, bcast_pe)
        for (c0, c1, pm, ps_) in rng3:
            OP("v", lambda v, s=sl, a=c0, b=c1, p=pm: v.scalar_tensor_tensor(
                out=tmpx[:, a:b], in0=pf[p][:, 0:b - a], scalar=-1.0, in1=xld[s][:, a:b],
                op0=ALU.mult, op1=ALU.add))
            OP("v", lambda v, ii=i, a=c0, b=c1, p=ps_: v.tensor_tensor(
                out=xln[:, ii, a:b], in0=tmpx[:, a:b], in1=pf[p][:, 0:b - a], op=ALU.mult))
        INC("v", lambda v: v.engine_nop(), ve, 1)
    if debug:
        W("g", ve)
        for i in range(32):
            INC("g", lambda g, ii=i: g.dma_start(out=dbg["xln"][ii], in_=xln[:, ii, :]), ldg, 16)
    barrier()
    p0.close()
    if PHASE_LIMIT <= 0:
        return _finish(nc, ops, root)

    p1s = ExitStack()
    wbuf = [sbuf(p1s, f"wbuf{i}", [128, 32, 128], BF16) for i in range(2)]
    stage = [sbuf(p1s, f"stg{i}", [128, 1027], F32) for i in range(2)]
    cacc = sbuf(p1s, "cacc", [128, TOK], F32)
    spl = [sbuf(p1s, f"spl{i}", [128, TOK], BF16) for i in range(4)]
    WpT3 = WpT.rearrange("(a p) o -> p a o", p=128)
    obs = list(range(64, 145)) + list(range(64))
    pe_mm = {}; se_free = {}; ve_cv = {}
    slot_free = [0] * 4
    sp_ct = 0
    for oi, ob in enumerate(obs):
        sl = oi % 2
        is_xbc = 64 <= ob < 144
        is_dt = ob == 144
        is_gate = ob < 64
        if oi >= 2:
            W("g", pe, pe_mm[oi - 2])
        INC("g", lambda g, o=ob, s=sl: g.dma_start(
            out=wbuf[s][:, :, :], in_=WpT3[:, :, o * 128:(o + 1) * 128]), wp[sl], 16)
        W("t", wp[sl])
        if oi >= 2:
            W("t", se, se_free[oi - 2])
        pA, pB = (0, 1) if sl == 0 else (2, 3)
        for tb in range(2):
            pk = pA if tb == 0 else pB
            for k in range(32):
                fn_ = lambda t, s=sl, kk=k, p=pk, b=tb: t.matmul(
                    pf[p][:, 0:512], lhsT=wbuf[s][:, kk, :],
                    rhs=xln[:, kk, 3 + b * 512: 3 + (b + 1) * 512],
                    start=(kk == 0), stop=(kk == 31))
                if (not is_xbc) and tb == 1 and k == 31:
                    INC("t", fn_, pe, 1)
                else:
                    OP("t", fn_)
        if is_xbc:
            pH = 4 if sl == 0 else 5
            for k in range(32):
                fn_ = lambda t, s=sl, kk=k, p=pH: t.matmul(
                    pf[p][:, 0:128], lhsT=wbuf[s][:, kk, :], rhs=xln[:, kk, 0:128],
                    start=(kk == 0), stop=(kk == 31))
                if k == 31:
                    INC("t", fn_, pe, 1)
                else:
                    OP("t", fn_)
        pe_mm[oi] = pe.v
        W("s", pe, pe_mm[oi])
        if is_gate:
            ssl = sp_ct % 4
            W("s", sp[ssl], slot_free[ssl])
            OP("s", lambda s, p=pA, z=ssl: s.activation(out=spl[z][:, 0:512], in_=pf[p][:, 0:512], func=AF.Silu))
            INC("s", lambda s, p=pB, z=ssl: s.activation(out=spl[z][:, 512:1024], in_=pf[p][:, 0:512], func=AF.Silu), se, 1)
            se_free[oi] = se.v
            W("y", se, se.v)
            INC("y", lambda y, o=ob, z=ssl: y.dma_start(out=sp_gate[o, :, :], in_=spl[z][:, :]), sp[ssl], 16)
            slot_free[ssl] = sp[ssl].v
            sp_ct += 1
        elif is_dt:
            OP("s", lambda s, p=pA: s.copy(out=dtres[:, 0:512], in_=pf[p][:, 0:512]))
            INC("s", lambda s, p=pB: s.copy(out=dtres[:, 512:1024], in_=pf[p][:, 0:512]), se, 1)
            se_free[oi] = se.v
        else:
            ct = ob - 64
            if oi >= 2 and (64 <= obs[oi - 2] < 144):
                W("s", ve, ve_cv[oi - 2])
            pH = 4 if sl == 0 else 5
            OP("s", lambda s, p=pA, z=sl: s.copy(out=stage[z][:, 3:515], in_=pf[p][:, 0:512]))
            OP("s", lambda s, p=pB, z=sl: s.copy(out=stage[z][:, 515:1027], in_=pf[p][:, 0:512]))
            INC("s", lambda s, z=sl, p=pH: s.copy(out=stage[z][:, 0:3], in_=pf[p][:, 0:3]), se, 1)
            se_free[oi] = se.v
            W("v", se, se.v)
            OP("v", lambda v, z=sl, c=ct: v.tensor_scalar(
                out=cacc[:, :], in0=stage[z][:, 0:TOK], scalar1=convw_s[:, c * 4:c * 4 + 1],
                scalar2=convb_s[:, c:c + 1], op0=ALU.mult, op1=ALU.add))
            for k in range(1, 4):
                OP("v", lambda v, z=sl, c=ct, kk=k: v.scalar_tensor_tensor(
                    out=cacc[:, :], in0=stage[z][:, kk:kk + TOK],
                    scalar=convw_s[:, c * 4 + kk:c * 4 + kk + 1], in1=cacc[:, :],
                    op0=ALU.mult, op1=ALU.add))
            INC("v", lambda v: v.engine_nop(), ve, 1)
            ve_cv[oi] = ve.v
            W("s", ve, ve.v)
            if ct < 64:
                ssl = sp_ct % 4
                W("s", sp[ssl], slot_free[ssl])
                INC("s", lambda s, z=ssl: s.activation(out=spl[z][:, :], in_=cacc[:, :], func=AF.Silu), se, 1)
                W("y", se, se.v)
                INC("y", lambda y, c=ct, z=ssl: y.dma_start(out=sp_hs[c, :, :], in_=spl[z][:, :]), sp[ssl], 16)
                slot_free[ssl] = sp[ssl].v
                sp_ct += 1
            else:
                INC("s", lambda s, c=ct: s.activation(out=BCr[:, c - 64, :], in_=cacc[:, :], func=AF.Silu), se, 1)
    if debug:
        W("g", se)
        for i in range(16):
            INC("g", lambda g, ii=i: g.dma_start(out=dbg["xbc"][ii], in_=BCr[:, ii, :]), ldg, 16)
    barrier()
    p1s.close()
    xlnS.close()
    if PHASE_LIMIT <= 1:
        return _finish(nc, ops, root)

    p2meta = ExitStack()
    negcumT = sbuf(p2meta, "negcumT", [128, NCH * 2, 128], F32, side="right")
    dtcT = sbuf(p2meta, "dtcT", [128, NCH * 2, 128], BF16, side="right")
    dtdlT = sbuf(p2meta, "dtdlT", [128, NCH * 2, 128], BF16, side="right")
    clbc = sbuf(p2meta, "clbc", [128, NCH, 128], F32, side="right")
    Bc_all = sbuf(p2meta, "Bc_all", [128, NCH, 2, G, 128], BF16, side="right")
    S_run = sbuf(p2meta, "S_run", [128, INTER], F32, side="right")
    tcum = sbuf(p2meta, "tcum", [128, 1], F32, side="right")
    OP("v", lambda v: v.memset(S_run[:, :], 0.0))
    OP("v", lambda v: v.memset(tcum[:, :], 0.0))
    INC("v", lambda v: v.engine_nop(), ve, 1)
    p2a = ExitStack()
    dtc_s = sbuf(p2a, "dtc_s", [128, TOK], F32)
    cum_all = sbuf(p2a, "cum_all", [128, NCH, CS], F32)
    t0b = sbuf(p2a, "t0b", [128, TOK], F32)
    cT_sb = sbuf(p2a, "cT_sb", [128, 2, 128], F32)
    dAT_sb = sbuf(p2a, "dAT_sb", [128, 2, 128], F32)
    t0row = sbuf(p2a, "t0row", [1, 128], F32)
    clT = sbuf(p2a, "clT", [1, 128], F32)
    misc = sbuf(p2a, "misc", [128, 256], F32)
    hsg = [sbuf(p2a, f"hsg{i}", [128, 8, 128], BF16) for i in range(2)]
    xT_sb = sbuf(p2a, "xT_sb", [128, 64, 128], BF16)
    dlx2 = sbuf(p2a, "dlx2", [128, 2, 64, 128], BF16)

    OP("v", lambda v: v.tensor_scalar(out=t0b[:, :], in0=dtres[:, :], scalar1=dtb_s[:, 0:1],
                                      scalar2=None, op0=ALU.add))
    INC("v", lambda v: v.engine_nop(), ve, 1)
    W("s", ve)
    OP("s", lambda s: s.activation(out=dtc_s[:, :], in_=t0b[:, :], func=AF.Abs))
    OP("s", lambda s: s.activation(out=dtc_s[:, :], in_=dtc_s[:, :], func=AF.Exp, scale=-1.0))
    OP("s", lambda s: s.activation(out=dtc_s[:, :], in_=dtc_s[:, :], func=AF.Ln, bias=1.0))
    INC("s", lambda s: s.activation(out=t0b[:, :], in_=t0b[:, :], func=AF.Relu), se, 1)
    W("v", se)
    INC("v", lambda v: v.tensor_tensor(out=dtc_s[:, :], in0=dtc_s[:, :], in1=t0b[:, :], op=ALU.add), ve, 1)
    if debug:
        W("y", ve)
        INC("y", lambda y: y.dma_start(out=dbg["dtc"], in_=dtc_s[:, :]), ldy, 16)
    INC("v", lambda v: v.tensor_scalar(out=t0b[:, :], in0=dtc_s[:, :], scalar1=a_col[:, 0:1],
                                       scalar2=-1.0, op0=ALU.mult, op1=ALU.mult), ve, 1)
    dA = t0b

    def tr_f32(src_ap, dst_ap, pbank):
        P_, Fc = src_ap.shape[0], src_ap.shape[1]
        W("t", ve, ve.v)
        INC("t", lambda t, s=src_ap, p=pbank, f=Fc, pp=P_: t.transpose(
            pf[p][0:f, 0:pp], s, idf[:, :]), pe, 1)
        W("v", pe)
        INC("v", lambda v, d=dst_ap, p=pbank, f=Fc, pp=P_: v.tensor_copy(
            out=d, in_=pf[p][0:f, 0:pp]), ve, 1)

    for c in range(NCH):
        for jb in range(2):
            tr_f32(dA[:, c * CS + jb * 128: c * CS + (jb + 1) * 128], dAT_sb[:, jb, :], 4)
        W("t", ve)
        OP("t", lambda t: t.matmul(pf[5][0:1, 0:128], lhsT=onescf[:, :], rhs=dAT_sb[:, 0, :], start=True, stop=True))
        INC("t", lambda t: t.matmul(pf[0][:, 0:128], lhsT=triu[:, :], rhs=dAT_sb[:, 0, :], start=True, stop=True), pe, 1)
        W("v", pe)
        OP("v", lambda v: v.tensor_copy(out=cT_sb[:, 0, :], in_=pf[0][:, 0:128]))
        INC("v", lambda v: v.tensor_copy(out=t0row[:, :], in_=pf[5][0:1, 0:128]), ve, 1)
        W("t", ve)
        OP("t", lambda t: t.matmul(pf[1][:, 0:128], lhsT=triu[:, :], rhs=dAT_sb[:, 1, :], start=True, stop=False))
        INC("t", lambda t: t.matmul(pf[1][:, 0:128], lhsT=ones1f[:, :], rhs=t0row[:, :], start=False, stop=True), pe, 1)
        W("v", pe)
        INC("v", lambda v: v.tensor_copy(out=cT_sb[:, 1, :], in_=pf[1][:, 0:128]), ve, 1)
        OP("v", lambda v, cc=c: v.tensor_scalar(out=negcumT[:, cc * 2, :], in0=cT_sb[:, 0, :],
                                                scalar1=-1.0, scalar2=None, op0=ALU.mult))
        INC("v", lambda v, cc=c: v.tensor_scalar(out=negcumT[:, cc * 2 + 1, :], in0=cT_sb[:, 1, :],
                                                 scalar1=-1.0, scalar2=None, op0=ALU.mult), ve, 1)
        for jb in range(2):
            tr_f32(cT_sb[:, jb, :], cum_all[:, c, jb * 128:(jb + 1) * 128], 0)
        W("y", ve)
        INC("y", lambda y, cc=c: y.dma_start(out=cumd[cc, :, :], in_=cum_all[:, cc, :]), ldy, 16)
        W("s", ve)
        INC("s", lambda s, cc=c: s.activation(out=misc[:, :], in_=cum_all[:, cc, :], func=AF.Exp,
                                              scale=-1.0, bias=cum_all[:, cc, 255:256]), se, 1)
        W("v", se)
        INC("v", lambda v, cc=c: v.tensor_tensor(out=misc[:, :], in0=misc[:, :],
                                                 in1=dtc_s[:, cc * CS:(cc + 1) * CS], op=ALU.mult), ve, 1)
        for jb in range(2):
            tr_f32(dtc_s[:, c * CS + jb * 128: c * CS + (jb + 1) * 128], dtcT[:, c * 2 + jb, :], 0)
            tr_f32(misc[:, jb * 128:(jb + 1) * 128], dtdlT[:, c * 2 + jb, :], 1)
        tr_f32(cum_all[:, c, 255:256], clT[:, :], 0)
        W("s", ve)
        INC("s", lambda s: s.activation(out=clT[:, :], in_=clT[:, :], func=AF.Exp), se, 1)
        W("t", se)
        INC("t", lambda t: t.matmul(pf[2][:, 0:128], lhsT=ones1f[:, :], rhs=clT[:, :], start=True, stop=True), pe, 1)
        W("v", pe)
        INC("v", lambda v, cc=c: v.tensor_copy(out=clbc[:, cc, :], in_=pf[2][:, 0:128]), ve, 1)
        INC("v", lambda v, cc=c: v.tensor_tensor(out=tcum[:, :], in0=tcum[:, :],
                                                 in1=cum_all[:, cc, 255:256], op=ALU.add), ve, 1)
        for g_i in range(G):
            for jb in range(2):
                W("t", ve, ve.v)
                INC("t", lambda t, gg=g_i, cc=c, j=jb: t.transpose(
                    pb[0][:, 0:128], BCr[:, gg, cc * CS + j * 128: cc * CS + (j + 1) * 128], idb[:, :]), pe, 1)
                W("v", pe)
                INC("v", lambda v, gg=g_i, cc=c, j=jb: v.tensor_copy(
                    out=Bc_all[:, cc, j, gg, :], in_=pb[0][:, 0:128]), ve, 1)
        for jb in range(2):
            for g8 in range(8):
                sl = g8 % 2
                W("y", ve, ve.v)
                INC("y", lambda y, cc=c, gg=g8, s=sl, j=jb: y.dma_start(
                    out=hsg[s][:, :, :],
                    in_=sp_hs.rearrange("a p t -> p a t")[:, gg * 8:(gg + 1) * 8,
                                                          cc * CS + j * 128: cc * CS + (j + 1) * 128]), rd[sl], 16)
                W("t", rd[sl])
                if c > 0 or jb > 0:
                    W("t", xw[0], xw[0].v)
                    W("t", xw[1], xw[1].v)
                for ci in range(8):
                    ct = g8 * 8 + ci
                    pbk = ci % 2
                    W("t", ve, ve.v - 1 if ci >= 2 else ve.v - 1)
                    INC("t", lambda t, s=sl, i2=ci, p=pbk: t.transpose(
                        pb[p][:, 0:128], hsg[s][:, i2, :], idb[:, :]), pe, 1)
                    W("v", pe)
                    INC("v", lambda v, cct=ct, p=pbk: v.tensor_copy(
                        out=xT_sb[:, cct, :], in_=pb[p][:, 0:128]), ve, 1)
            def dlx_fn(v, cc=c, j=jb):
                base = dtdlT[:, cc * 2 + j, :]
                st = base.ap[1][0]
                ina = _ap3(base, [[st * 2, 64], [st, 2], [0, 64]])
                xb_ = xT_sb[:, :, :]
                xst = xb_.ap[2][0]
                xa = _ap3(xb_, [[xb_.ap[1][0], 64], [64 * xst, 2], [xst, 64]])
                db_ = dlx2[:, j, :, :]
                dst_ = db_.ap[2][0]
                da = _ap3(db_, [[db_.ap[1][0], 64], [64 * dst_, 2], [dst_, 64]])
                return v.tensor_tensor(out=da, in0=xa, in1=ina, op=ALU.mult)
            W("v", ve, ve.v)
            INC("v", dlx_fn, ve, 1)
            W("y", ve, ve.v)
            INC("y", lambda y, cc=c, j=jb: y.dma_start(out=sp_xT[cc, j, :, :], in_=xT_sb[:, :, :]), xw[jb], 16)
        for g_i in range(G):
            W("t", ve, ve.v)
            for jb in range(2):
                for hh in range(16):
                    h_g = g_i * 16 + hh
                    q = hh // 4
                    fn = lambda t, cc=c, j=jb, gg=g_i, h2=hh, qq=q, hg=h_g: t.matmul(
                        pf[qq][:, (h2 % 4) * 64:(h2 % 4) * 64 + 64],
                        lhsT=Bc_all[:, cc, j, gg, :],
                        rhs=dlx2[:, j, hg // 2, (hg % 2) * 64:(hg % 2) * 64 + 64],
                        start=(j == 0 and h2 % 4 == 0), stop=(j == 1 and h2 % 4 == 3))
                    if jb == 1 and hh == 15:
                        INC("t", fn, pe, 1)
                    else:
                        OP("t", fn)
            W("v", pe)
            for hh in range(16):
                h_g = g_i * 16 + hh
                fn = lambda v, cc=c, hg=h_g, h2=hh: v.scalar_tensor_tensor(
                    out=S_run[:, hg * 64:(hg + 1) * 64], in0=S_run[:, hg * 64:(hg + 1) * 64],
                    scalar=clbc[:, cc, hg:hg + 1], in1=pf[h2 // 4][:, (h2 % 4) * 64:(h2 % 4) * 64 + 64],
                    op0=ALU.mult, op1=ALU.add)
                if hh == 15:
                    INC("v", fn, ve, 1)
                else:
                    OP("v", fn)
    if debug:
        W("y", ve)
        INC("y", lambda y: y.dma_start(out=dbg["T"], in_=S_run[:, :]), ldy, 16)
        INC("y", lambda y: y.dma_start(out=dbg["cum"], in_=cum_all[:, :, :]), ldy, 16)
    tr_f32(tcum[:, :], clT[:, :], 0)
    W("s", ve)
    INC("s", lambda s: s.activation(out=clT[:, :], in_=clT[:, :], func=AF.Exp), se, 1)
    W("t", se)
    INC("t", lambda t: t.matmul(pf[2][:, 0:128], lhsT=ones1f[:, :], rhs=clT[:, :], start=True, stop=True), pe, 1)
    W("v", pe)
    INC("v", lambda v: v.tensor_copy(out=misc[:, 0:128], in_=pf[2][:, 0:128]), ve, 1)
    W("y", ve)
    INC("y", lambda y: y.dma_start(out=cst_in[:, 0:INTER], in_=S_run[:, :]), ldy, 16)
    INC("y", lambda y: y.dma_start(out=cst_in[:, INTER:INTER + 128], in_=misc[:, 0:128]), ldy, 16)
    W("g", ldy)
    INC("g", lambda g: g.collective_compute(
        "AllGather", ALU.bypass, replica_groups=[[0, 1, 2, 3], [4, 5, 6, 7]],
        ins=[cst_in_t.ap().opt()], outs=[cst_out_t.ap().opt()]), scc, 1)
    barrier()
    p2a.close()
    if PHASE_LIMIT <= 2:
        return _finish(nc, ops, root)

    p2b = ExitStack()
    tmpT = sbuf(p2b, "tmpT", [128, INTER], F32)
    dkb = sbuf(p2b, "dkb", [128, 128], F32)
    W("v", ldy)
    OP("v", lambda v: v.memset(S_run[:, :], 0.0))
    INC("v", lambda v: v.engine_nop(), ve, 1)
    for k in range(3):
        W("y", scc)
        W("y", ve, ve.v)
        INC("y", lambda y, kk=k: y.dma_start(out=tmpT[:, :], in_=cst_out[kk * 128:(kk + 1) * 128, 0:INTER]), rd[0], 16)
        INC("y", lambda y, kk=k: y.dma_start(out=dkb[:, :], in_=cst_out[kk * 128:(kk + 1) * 128, INTER:INTER + 128]), rd[0], 16)
        W("v", rd[0])
        OP("v", lambda v, kk=k: v.tensor_scalar(out=tmpT[:, :], in0=tmpT[:, :],
                                                scalar1=cmsk_s[:, kk:kk + 1], scalar2=None, op0=ALU.mult))
        OP("v", lambda v, kk=k: v.tensor_scalar(out=dkb[:, :], in0=dkb[:, :], scalar1=1.0,
                                                scalar2=cmsk_s[:, kk:kk + 1], op0=ALU.subtract, op1=ALU.mult))
        OP("v", lambda v: v.tensor_scalar(out=dkb[:, :], in0=dkb[:, :], scalar1=1.0, scalar2=None, op0=ALU.add))
        for h_ in range(128):
            fn = lambda v, hh=h_: v.scalar_tensor_tensor(
                out=S_run[:, hh * 64:(hh + 1) * 64], in0=S_run[:, hh * 64:(hh + 1) * 64],
                scalar=dkb[:, hh:hh + 1], in1=tmpT[:, hh * 64:(hh + 1) * 64], op0=ALU.mult, op1=ALU.add)
            if h_ == 127:
                INC("v", fn, ve, 1)
            else:
                OP("v", fn)
    if debug:
        W("y", ve)
        INC("y", lambda y: y.dma_start(out=dbg["Sent"], in_=S_run[:, :]), ldy, 16)
    barrier()
    p2b.close()
    if PHASE_LIMIT <= 3:
        return _finish(nc, ops, root)

    p2c = ExitStack()
    xtg = [sbuf(p2c, f"xtg{i}", [128, 2, 8, 128], BF16) for i in range(2)]
    bcg = sbuf(p2c, "bcg", [128, 16, 256], F32)
    mk1 = sbuf(p2c, "mk1", [128, 16, 128], F32)
    dxg = sbuf(p2c, "dxg", [128, 2, 8, 128], BF16)
    dlxg = sbuf(p2c, "dlxg", [128, 2, 8, 128], BF16)
    hs2 = sbuf(p2c, "hs2", [128, 8, CS], BF16)
    gt2 = sbuf(p2c, "gt2", [128, 8, CS], BF16)
    sbf = [sbuf(p2c, f"sbf{i}", [128, 1024], BF16) for i in range(2)]
    e_b = [sbuf(p2c, f"e_b{i}", [128, 256], BF16) for i in range(2)]
    s0b = [sbuf(p2c, f"s0b{i}", [128, 256], BF16) for i in range(2)]
    s1b = [sbuf(p2c, f"s1b{i}", [128, 128], BF16) for i in range(2)]
    s0m = sbuf(p2c, "s0m", [128, 16, 256], BF16)
    s1m = sbuf(p2c, "s1m", [128, 16, 128], BF16)
    cdm = sbuf(p2c, "cdm", [128, 16, 256], BF16)
    sqg = sbuf(p2c, "sqg", [128, 8, 256], BF16)
    ygst = sbuf(p2c, "ygst", [128, 8, 256], BF16)
    ggst = sbuf(p2c, "ggst", [128, 8, 256], BF16)
    rmsrow = sbuf(p2c, "rmsrow", [1, 256], F32)
    rms2 = sbuf(p2c, "rms2", [1, 256], F32)
    rms3 = sbuf(p2c, "rms3", [1, 256], F32)

    hsR = sp_hs.rearrange("a p t -> p a t")
    gtR = sp_gate.rearrange("a p t -> p a t")
    for c in range(NCH):
        for g_i in range(G):
            gp = (c * G + g_i) % 2
            sA = 4 + gp
            sl = g_i % 2
            W("y", ve, ve.v)
            INC("y", lambda y, cc=c, gg=g_i: y.dma_start(
                out=hs2[:, :, :], in_=hsR[:, gg * 8:(gg + 1) * 8, cc * CS:(cc + 1) * CS]), rd[2], 16)
            INC("y", lambda y, cc=c, gg=g_i: y.dma_start(
                out=gt2[:, :, :], in_=gtR[:, gg * 8:(gg + 1) * 8, cc * CS:(cc + 1) * CS]), rd[2], 16)
            for jb in range(2):
                INC("y", lambda y, cc=c, gg=g_i, j=jb, z=sl: y.dma_start(
                    out=xtg[z][:, j, :, :],
                    in_=sp_xT[cc, j, :, gg * 1024:(gg + 1) * 1024]), rd[3], 16)
            W("y", se, se.v)
            def bcg_load(y, cc=c, gg=g_i):
                base = bass.AP(cumd.tensor, cc * 128 * CS + gg * 16 * CS, [[0, 128], [CS, 16], [1, CS]])
                return y.dma_start(out=bcg[:, :, :], in_=base)
            INC("y", bcg_load, sbc, 16)
            W("v", rd[3])
            W("v", pe, pe.v)
            for jb in range(2):
                def dxfn(v, cc=c, gg=g_i, j=jb, dst=dxg, src=dtcT, z=sl):
                    base = src[:, cc * 2 + j, gg * 16: gg * 16 + 16]
                    st = base.ap[1][0]
                    ina = _ap3(base, [[st * 2, 8], [st, 2], [0, 64]])
                    xb_ = xtg[z][:, j, :, :]
                    xst = xb_.ap[2][0]
                    xa = _ap3(xb_, [[xb_.ap[1][0], 8], [64 * xst, 2], [xst, 64]])
                    db_ = dst[:, j, :, :]
                    dstp = db_.ap[2][0]
                    da = _ap3(db_, [[db_.ap[1][0], 8], [64 * dstp, 2], [dstp, 64]])
                    v.tensor_tensor(out=da, in0=xa, in1=ina, op=ALU.mult)
                OP("v", lambda v, cc=c, gg=g_i, j=jb, z=sl: dxfn(v, cc, gg, j, dxg, dtcT, z))
                OP("v", lambda v, cc=c, gg=g_i, j=jb, z=sl: dxfn(v, cc, gg, j, dlxg, dtdlT, z))
            INC("v", lambda v: v.engine_nop(), ve, 1)
            W("t", ve, ve.v)
            OP("t", lambda t, cc=c, gg=g_i, p=sA: t.matmul(
                pf[p][:, 0:256], lhsT=BCr[:, gg, cc * CS: cc * CS + 128],
                rhs=BCr[:, 8 + gg, cc * CS:(cc + 1) * CS], start=True, stop=False))
            INC("t", lambda t, cc=c, gg=g_i, p=sA: t.matmul(
                pf[p][:, 256:512], lhsT=BCr[:, gg, cc * CS + 128: cc * CS + 256],
                rhs=BCr[:, 8 + gg, cc * CS:(cc + 1) * CS], start=False, stop=True), pe, 1)
            pe_scores = pe.v
            for jb in range(2):
                for hh in range(16):
                    h_g = g_i * 16 + hh
                    q = hh // 4
                    fn = lambda t, cc=c, j=jb, gg=g_i, h2=hh, qq=q, hg=h_g: t.matmul(
                        pf[qq][:, (h2 % 4) * 64:(h2 % 4) * 64 + 64],
                        lhsT=Bc_all[:, cc, j, gg, :],
                        rhs=dlxg[:, j, h2 // 2, (h2 % 2) * 64:(h2 % 2) * 64 + 64],
                        start=(j == 0 and h2 % 4 == 0), stop=(j == 1 and h2 % 4 == 3))
                    if jb == 1 and hh == 15:
                        INC("t", fn, pe, 1)
                    else:
                        OP("t", fn)
            pe_states = pe.v
            W("v", pe, pe_states)
            INC("v", lambda v, gg=g_i, s=sl: v.tensor_copy(
                out=sbf[s][:, :], in_=S_run[:, gg * 1024:(gg + 1) * 1024]), ve, 1)
            for hh in range(16):
                h_g = g_i * 16 + hh
                fn = lambda v, cc=c, hg=h_g, h2=hh: v.scalar_tensor_tensor(
                    out=S_run[:, hg * 64:(hg + 1) * 64], in0=S_run[:, hg * 64:(hg + 1) * 64],
                    scalar=clbc[:, cc, hg:hg + 1], in1=pf[h2 // 4][:, (h2 % 4) * 64:(h2 % 4) * 64 + 64],
                    op0=ALU.mult, op1=ALU.add)
                if hh == 15:
                    INC("v", fn, ve, 1)
                else:
                    OP("v", fn)
            ve_scan = ve.v
            W("s", sbc)
            cd_done = {}
            for hh in range(16):
                if hh >= 2:
                    W("s", ve, cd_done[hh - 2])
                INC("s", lambda s, h2=hh, z=hh % 2: s.activation(
                    out=e_b[z][:, :], in_=bcg[:, h2, :], func=AF.Exp), se, 1)
                W("v", se, se.v)
                INC("v", lambda v, cc=c, gg=g_i, h2=hh, z=hh % 2: v.tensor_tensor(
                    out=cdm[:, h2, :], in0=e_b[z][:, :], in1=BCr[:, 8 + gg, cc * CS:(cc + 1) * CS],
                    op=ALU.mult), ve, 1)
                cd_done[hh] = ve.v
            se_e = se.v
            W("v", se, se_e)
            def m0fn(v):
                b_ = bcg[:, :, 0:128]
                ina = _ap3(mtri[:, :], [[0, 16], [mtri[:, :].ap[1][0], 128]])
                return v.tensor_tensor(out=b_, in0=b_, in1=ina, op=ALU.add)
            INC("v", m0fn, ve, 1)
            def m1fn_(v):
                ina = _ap3(mtri[:, :], [[0, 16], [mtri[:, :].ap[1][0], 128]])
                return v.tensor_tensor(out=mk1[:, :, :], in0=bcg[:, :, 128:256], in1=ina, op=ALU.add)
            INC("v", m1fn_, ve, 1)
            W("v", ve)
            INC("v", lambda v: v.tensor_copy(out=dxg[0:1, 0, 0, 0:1], in_=dxg[0:1, 0, 0, 0:1]), ve, 1)
            ve_m0 = ve.v
            tt0_done = {}
            W("s", ve, ve_m0)
            for hh in range(16):
                if hh >= 2:
                    W("s", ve, tt0_done[hh - 2])
                INC("s", lambda s, cc=c, gg=g_i, h2=hh, z=hh % 2: s.activation(
                    out=s0b[z][:, :], in_=bcg[:, h2, :], func=AF.Exp,
                    bias=negcumT[:, cc * 2, gg * 16 + h2: gg * 16 + h2 + 1]), se, 1)
                W("v", se, se.v)
                if hh == 0:
                    W("v", pe, pe_scores)
                INC("v", lambda v, z=hh % 2, h2=hh, p=sA: v.tensor_tensor(
                    out=s0m[:, h2, :], in0=s0b[z][:, :], in1=pf[p][:, 0:256], op=ALU.mult), ve, 1)
                tt0_done[hh] = ve.v
            tt1_done = {}
            for hh in range(16):
                if hh >= 2:
                    W("s", ve, tt1_done[hh - 2])
                INC("s", lambda s, cc=c, gg=g_i, h2=hh, z=hh % 2: s.activation(
                    out=s1b[z][:, :], in_=mk1[:, h2, :], func=AF.Exp,
                    bias=negcumT[:, cc * 2 + 1, gg * 16 + h2: gg * 16 + h2 + 1]), se, 1)
                W("v", se, se.v)
                INC("v", lambda v, z=hh % 2, h2=hh, p=sA: v.tensor_tensor(
                    out=s1m[:, h2, :], in0=s1b[z][:, :], in1=pf[p][:, 384:512], op=ALU.mult), ve, 1)
                tt1_done[hh] = ve.v
            pair_ev = {}
            for hh in range(16):
                W("t", ve, tt1_done[hh])
                if hh == 0:
                    W("t", ve, ve_scan)
                yb = (hh // 2) % 2
                col = (hh % 2) * 64
                pr = hh // 2
                if hh % 2 == 0 and pr >= 2:
                    W("t", ve, pair_ev[pr - 2])
                OP("t", lambda t, h2=hh, b=yb, cl=col: t.matmul(
                    pf[b][cl:cl + 64, 0:256],
                    lhsT=dxg[:, 0, h2 // 2, (h2 % 2) * 64:(h2 % 2) * 64 + 64],
                    rhs=s0m[:, h2, :], start=True, stop=False,
                    tile_position=(0, cl)))
                OP("t", lambda t, h2=hh, b=yb, cl=col: t.matmul(
                    pf[b][cl:cl + 64, 128:256],
                    lhsT=dxg[:, 1, h2 // 2, (h2 % 2) * 64:(h2 % 2) * 64 + 64],
                    rhs=s1m[:, h2, :], start=False, stop=False, tile_position=(0, cl)))
                fn3 = lambda t, h2=hh, b=yb, cl=col, s=sl: t.matmul(
                    pf[b][cl:cl + 64, 0:256],
                    lhsT=sbf[s][:, (h2 * 64):(h2 + 1) * 64],
                    rhs=cdm[:, h2, :], start=False, stop=True, tile_position=(0, cl))
                if hh % 2 == 1:
                    INC("t", fn3, pe, 1)
                else:
                    OP("t", fn3)
                if hh % 2 == 1:
                    W("v", pe, pe.v)
                    if hh == 1:
                        W("v", rd[2])
                    OP("v", lambda v, gg=g_i, p2=pr, b=yb: v.scalar_tensor_tensor(
                        out=ygst[:, p2, :], in0=hs2[:, p2, :], scalar=dexp_s[:, gg * 8 + p2:gg * 8 + p2 + 1],
                        in1=pf[b][:, 0:256], op0=ALU.mult, op1=ALU.add))
                    INC("v", lambda v, p2=pr: v.tensor_tensor(
                        out=ygst[:, p2, :], in0=ygst[:, p2, :], in1=gt2[:, p2, :], op=ALU.mult), ve, 1)
                    pair_ev[pr] = ve.v
                    W("s", ve, ve.v)
                    INC("s", lambda s_, p2=pr: s_.activation(
                        out=sqg[:, p2, :], in_=ygst[:, p2, :], func=AF.Square), se, 1)
            se_sq = se.v
            W("t", se, se_sq)
            W("t", ve, ve.v)
            for p2 in range(8):
                fn4 = lambda t, pp=p2, p=sA: t.matmul(
                    pf[p][0:1, 0:256], lhsT=onescb[:, :], rhs=sqg[:, pp, :],
                    start=(pp == 0), stop=(pp == 7))
                if p2 == 7:
                    INC("t", fn4, pe, 1)
                else:
                    OP("t", fn4)
            W("v", pe, pe.v)
            OP("v", lambda v, p=sA: v.tensor_copy(out=rmsrow[:, :], in_=pf[p][0:1, 0:256]))
            OP("v", lambda v: v.tensor_scalar(out=rmsrow[:, :], in0=rmsrow[:, :],
                                              scalar1=1.0 / 1024.0, scalar2=EPS, op0=ALU.mult, op1=ALU.add))
            INC("v", lambda v: v.engine_nop(), ve, 1)
            W("s", ve, ve.v)
            OP("s", lambda s: s.activation(out=rms2[:, :], in_=rmsrow[:, :], func=AF.Ln))
            INC("s", lambda s: s.activation(out=rms2[:, :], in_=rms2[:, :], func=AF.Exp, scale=-0.5), se, 1)
            W("v", se, se.v)
            OP("v", lambda v: v.tensor_tensor(out=rms3[:, :], in0=rms2[:, :], in1=rms2[:, :], op=ALU.mult))
            OP("v", lambda v: v.scalar_tensor_tensor(out=rms3[:, :], in0=rms3[:, :], scalar=-0.5,
                                                     in1=rmsrow[:, :], op0=ALU.mult, op1=ALU.mult))
            OP("v", lambda v: v.tensor_scalar(out=rms3[:, :], in0=rms3[:, :], scalar1=1.5,
                                              scalar2=None, op0=ALU.add))
            INC("v", lambda v: v.tensor_tensor(out=rms2[:, :], in0=rms2[:, :], in1=rms3[:, :], op=ALU.mult), ve, 1)
            W("t", ve, ve.v)
            INC("t", lambda t, p=sA: t.matmul(pf[p][:, 0:256], lhsT=ones1f[:, :], rhs=rms2[:, :],
                                              start=True, stop=True), pe, 1)
            pe_rb = pe.v
            W("v", pe, pe_rb)
            W("v", gw[0], gw[0].v)
            for pr in range(8):
                fn = lambda v, gg=g_i, p2=pr, p=sA: v.scalar_tensor_tensor(
                    out=ggst[:, p2, :], in0=ygst[:, p2, :], scalar=nwexp_s[:, gg * 8 + p2:gg * 8 + p2 + 1],
                    in1=pf[p][:, 0:256], op0=ALU.mult, op1=ALU.mult)
                if pr == 7:
                    INC("v", fn, ve, 1)
                else:
                    OP("v", fn)
            W("y", ve, ve.v)
            INC("y", lambda y, cc=c, gg=g_i: y.dma_start(
                out=sp_g.rearrange("c a p t -> p c a t")[:, cc, gg * 8:(gg + 1) * 8, :],
                in_=ggst[:, :, :]), gw[0], 16)
    if debug:
        W("g", gw[0])
        W("g", ve)
        for cc_ in range(NCH):
            INC("g", lambda g, cc=cc_: g.dma_start(out=dbg["g"][cc], in_=sp_g[cc]), ldg, 16)
    barrier()
    p2c.close()
    p2meta.close()
    if PHASE_LIMIT <= 4:
        return _finish(nc, ops, root)

    p3w = ExitStack()
    g_res = sbuf(p3w, "g_res", [128, 64, TOK], BF16)
    wo = [sbuf(p3w, f"wo{i}", [128, 32, 128], BF16) for i in range(2)]
    attb = [sbuf(p3w, f"attb{i}", [128, TOK], BF16) for i in range(2)]
    for c in range(NCH):
        INC("y", lambda y, cc=c: y.dma_start(
            out=g_res[:, :, cc * CS:(cc + 1) * CS],
            in_=sp_g.rearrange("c a p t -> p c a t")[:, cc, :, :]), ldy, 16)
    WoT3 = WoT.rearrange("(a p) o -> p a o", p=128)
    g_ld = ldy.v
    pe_mm3 = {}; se_ev3 = {}
    pe_half = {}
    for ob in range(32):
        for hf in range(2):
            sl = (2 * ob + hf) % 2
            key = 2 * ob + hf
            if key >= 2:
                W("g", pe, pe_half[key - 2])
            INC("g", lambda g, o=ob, h=hf, s=sl: g.dma_start(
                out=wo[s][:, :, :], in_=WoT3[:, h * 32:(h + 1) * 32, o * 128:(o + 1) * 128]), wp[sl], 16)
        sl = 0
        if ob >= 2:
            W("t", se, se_ev3[ob - 2])
        pA, pB = (0, 1) if ob % 2 == 0 else (2, 3)
        for hf in range(2):
            s_ = (2 * ob + hf) % 2
            W("t", wp[s_], wp[s_].v if True else 0)
            if ob == 0 and hf == 0:
                W("t", ldy, g_ld)
            for tb in range(2):
                pk = pA if tb == 0 else pB
                for k in range(32):
                    fn_ = lambda t, s=s_, kk=k, h=hf, p=pk, b=tb: t.matmul(
                        pf[p][:, 0:512], lhsT=wo[s][:, kk, :],
                        rhs=g_res[:, h * 32 + kk, b * 512:(b + 1) * 512],
                        start=(h == 0 and kk == 0), stop=(h == 1 and kk == 31))
                    if tb == 1 and k == 31:
                        INC("t", fn_, pe, 1)
                    else:
                        OP("t", fn_)
            pe_half[2 * ob + hf] = pe.v
        pe_mm3[ob] = pe.v
        sl = ob % 2
        W("s", pe, pe_mm3[ob])
        if ob >= 2:
            W("s", sp[sl], sp[sl].v)
        OP("s", lambda s, p=pA, z=sl: s.copy(out=attb[z][:, 0:512], in_=pf[p][:, 0:512]))
        INC("s", lambda s, p=pB, z=sl: s.copy(out=attb[z][:, 512:1024], in_=pf[p][:, 0:512]), se, 1)
        se_ev3[ob] = se.v
        W("t", se, se.v)
        OP("t", lambda t, z=sl, f=(ob == 0), l=(ob == 31): t.matmul(
            pf[4][0:1, 0:512], lhsT=onescb[:, :], rhs=attb[z][:, 0:512], start=f, stop=l))
        INC("t", lambda t, z=sl, f=(ob == 0), l=(ob == 31): t.matmul(
            pf[5][0:1, 0:512], lhsT=onescb[:, :], rhs=attb[z][:, 512:1024], start=f, stop=l), pe, 1)
        pe_mm3[ob] = pe.v
        W("y", se, se_ev3[ob])
        INC("y", lambda y, o=ob, z=sl: y.dma_start(out=sp_att[o, :, :], in_=attb[z][:, :]), sp[sl], 16)
    if debug:
        for c_ in sp:
            W("g", c_)
        for i_ in range(32):
            INC("g", lambda g, ii=i_: g.dma_start(out=dbg["att"][ii], in_=sp_att[ii]), ldg, 16)
    barrier()
    p3w.close()
    xln2S = ExitStack()
    xln2 = sbuf(xln2S, "xln2", [128, 32, TOK], BF16, side="right")
    p3 = ExitStack()
    attb2 = [sbuf(p3, f"attb2_{i}", [128, TOK], BF16) for i in range(2)]
    xmidl = sbuf(p3, "xmidl", [128, TOK], F32)
    xlw = [sbuf(p3, f"xlw{i}", [128, TOK + 3], F32) for i in range(2)]
    atr = [sbuf(p3, f"atr{i}", [128, TOK], BF16) for i in range(2)]
    tmp3 = sbuf(p3, "tmp3", [128, TOK], F32)
    sq2 = sbuf(p3, "sq2", [128, TOK], BF16)
    for ob in range(32):
        sl = ob % 2
        if ob >= 2:
            W("y", se, se.v - 1)
        INC("y", lambda y, o=ob, z=sl: y.dma_start(out=attb2[z][:, :], in_=sp_att[o, :, :]), rd[sl], 16)
        W("s", rd[sl])
        INC("s", lambda s, z=sl: s.activation(out=sq2[:, :], in_=attb2[z][:, :], func=AF.Square), se, 1)
        W("t", se, se.v)
        OP("t", lambda t, f=(ob == 0), l=(ob == 31): t.matmul(
            pf[0][0:1, 0:512], lhsT=onescb[:, :], rhs=sq2[:, 0:512], start=f, stop=l))
        fn = lambda t, f=(ob == 0), l=(ob == 31): t.matmul(
            pf[1][0:1, 0:512], lhsT=onescb[:, :], rhs=sq2[:, 512:1024], start=f, stop=l)
        if ob == 31:
            INC("t", fn, pe, 1)
        else:
            OP("t", fn)
    W("v", pe)
    OP("v", lambda v: v.tensor_copy(out=rowA[:, 0:512], in_=pf[4][0:1, 0:512]))
    OP("v", lambda v: v.tensor_copy(out=rowA[:, 512:1024], in_=pf[5][0:1, 0:512]))
    OP("v", lambda v: v.tensor_copy(out=rowC[:, 0:512], in_=pf[0][0:1, 0:512]))
    OP("v", lambda v: v.tensor_copy(out=rowC[:, 512:1024], in_=pf[1][0:1, 0:512]))
    INC("v", lambda v: v.engine_nop(), ve, 1)
    stats_finalize(HID, 1024)
    if debug:
        W("y", ve)
        INC("y", lambda y: y.dma_start(out=dbg["r3a"][0], in_=rowA[:, :]), ldy, 16)
        INC("y", lambda y: y.dma_start(out=dbg["r3a"][1], in_=rowB[:, :]), ldy, 16)
    W("t", ve)
    for wi, (row, pA_, pB_) in enumerate([(rowA, 0, 1), (rowB, 2, 3)]):
        OP("t", lambda t, r=row, p=pA_: t.matmul(pf[p][:, 0:512], lhsT=ones1f[:, :], rhs=r[:, 0:512], start=True, stop=True))
        fn_ = lambda t, r=row, p=pB_: t.matmul(pf[p][:, 0:512], lhsT=ones1f[:, :], rhs=r[:, 512:1024], start=True, stop=True)
        if wi == 1:
            INC("t", fn_, pe, 1)
        else:
            OP("t", fn_)
    pe_b3 = pe.v
    for i in range(32):
        sl = i % 2
        if i >= 2:
            W("y", ve, ve.v - 1)
        INC("y", lambda y, ii=i, s=sl: y.dma_start(out=xlw[s][:, :], in_=xT3[:, ii, :]), rd[sl], 16)
        INC("y", lambda y, ii=i, s=sl: y.dma_start(out=atr[s][:, :], in_=sp_att[ii, :, :]), rd[sl], 16)
        W("v", rd[sl]); W("v", pe, pe_b3)
        for (c0, pm, pr_) in [(0, 0, 2), (512, 1, 3)]:
            OP("v", lambda v, s=sl, a=c0, p=pm: v.scalar_tensor_tensor(
                out=tmp3[:, a:a + 512], in0=pf[p][:, 0:512], scalar=-1.0, in1=atr[s][:, a:a + 512],
                op0=ALU.mult, op1=ALU.add))
            OP("v", lambda v, a=c0, p=pr_: v.tensor_tensor(
                out=tmp3[:, a:a + 512], in0=tmp3[:, a:a + 512], in1=pf[p][:, 0:512], op=ALU.mult))
            OP("v", lambda v, s=sl, a=c0: v.tensor_tensor(
                out=xmidl[:, a:a + 512], in0=tmp3[:, a:a + 512], in1=xlw[s][:, 3 + a:3 + a + 512], op=ALU.add))
        INC("v", lambda v: v.engine_nop(), ve, 1)
        W("y", ve, ve.v)
        INC("y", lambda y, ii=i: y.dma_start(out=sp_xmid[ii, :, :], in_=xmidl[:, :]), sp[sl], 16)
        W("t", ve, ve.v)
        OP("t", lambda t, f=(i == 0), l=(i == 31): t.matmul(
            pf[4][0:1, 0:512], lhsT=onescf[:, :], rhs=xmidl[:, 0:512], start=f, stop=l))
        fn = lambda t, f=(i == 0), l=(i == 31): t.matmul(
            pf[5][0:1, 0:512], lhsT=onescf[:, :], rhs=xmidl[:, 512:1024], start=f, stop=l)
        if i == 31:
            INC("t", fn, pe, 1)
        else:
            OP("t", fn)
        if debug:
            W("y", ve, ve.v)
            INC("y", lambda y, ii=i: y.dma_start(out=dbg["xmid"][ii], in_=xmidl[:, :]), ldy, 16)
    barrier()
    p3.close()
    if PHASE_LIMIT <= 5:
        return _finish(nc, ops, root)

    p3b = ExitStack()
    xmr = [sbuf(p3b, f"xmr{i}", [128, TOK], F32) for i in range(2)]
    sq4 = sbuf(p3b, "sq4", [128, TOK], F32)
    for i in range(32):
        sl = i % 2
        if i >= 2:
            W("y", se, se.v - 1)
        INC("y", lambda y, ii=i, s=sl: y.dma_start(out=xmr[s][:, :], in_=sp_xmid[ii, :, :]), rd[sl], 16)
        W("s", rd[sl])
        INC("s", lambda s, z=sl: s.activation(out=sq4[:, :], in_=xmr[z][:, :], func=AF.Square), se, 1)
        W("t", se, se.v)
        OP("t", lambda t, f=(i == 0), l=(i == 31): t.matmul(
            pf[2][0:1, 0:512], lhsT=onescf[:, :], rhs=sq4[:, 0:512], start=f, stop=l))
        fn = lambda t, f=(i == 0), l=(i == 31): t.matmul(
            pf[3][0:1, 0:512], lhsT=onescf[:, :], rhs=sq4[:, 512:1024], start=f, stop=l)
        if i == 31:
            INC("t", fn, pe, 1)
        else:
            OP("t", fn)
    W("v", pe)
    OP("v", lambda v: v.tensor_copy(out=rowA[:, 0:512], in_=pf[4][0:1, 0:512]))
    OP("v", lambda v: v.tensor_copy(out=rowA[:, 512:1024], in_=pf[5][0:1, 0:512]))
    OP("v", lambda v: v.tensor_copy(out=rowC[:, 0:512], in_=pf[2][0:1, 0:512]))
    OP("v", lambda v: v.tensor_copy(out=rowC[:, 512:1024], in_=pf[3][0:1, 0:512]))
    INC("v", lambda v: v.engine_nop(), ve, 1)
    stats_finalize(HID, 1024)
    if debug:
        W("y", ve)
        INC("y", lambda y: y.dma_start(out=dbg["r3b"][0], in_=rowA[:, :]), ldy, 16)
        INC("y", lambda y: y.dma_start(out=dbg["r3b"][1], in_=rowB[:, :]), ldy, 16)
    W("t", ve)
    for wi, (row, pA_, pB_) in enumerate([(rowA, 0, 1), (rowB, 2, 3)]):
        OP("t", lambda t, r=row, p=pA_: t.matmul(pf[p][:, 0:512], lhsT=ones1f[:, :], rhs=r[:, 0:512], start=True, stop=True))
        fn_ = lambda t, r=row, p=pB_: t.matmul(pf[p][:, 0:512], lhsT=ones1f[:, :], rhs=r[:, 512:1024], start=True, stop=True)
        if wi == 1:
            INC("t", fn_, pe, 1)
        else:
            OP("t", fn_)
    pe_b4 = pe.v
    for i in range(32):
        sl = i % 2
        if i >= 2:
            W("y", ve, ve.v - 1)
        else:
            W("y", se, se.v)
        INC("y", lambda y, ii=i, s=sl: y.dma_start(out=xmr[s][:, :], in_=sp_xmid[ii, :, :]), rd[sl], 16)
        W("v", rd[sl]); W("v", pe, pe_b4)
        for (c0, pm, pr_) in [(0, 0, 2), (512, 1, 3)]:
            OP("v", lambda v, s=sl, a=c0, p=pm: v.scalar_tensor_tensor(
                out=sq4[:, a:a + 512], in0=pf[p][:, 0:512], scalar=-1.0, in1=xmr[s][:, a:a + 512],
                op0=ALU.mult, op1=ALU.add))
            OP("v", lambda v, ii=i, a=c0, p=pr_: v.tensor_tensor(
                out=xln2[:, ii, a:a + 512], in0=sq4[:, a:a + 512], in1=pf[p][:, 0:512], op=ALU.mult))
        INC("v", lambda v: v.engine_nop(), ve, 1)
    barrier()
    p3b.close()
    if PHASE_LIMIT <= 6:
        return _finish(nc, ops, root)

    p4 = ExitStack()
    wmask_s = sbuf(p4, "wmask_s", [1, TOK], F32)
    INC("g", lambda g: g.dma_start(out=wmask_s[:, :], in_=wmaskd), ldg, 16)
    dump = sbuf(p4, "dump", [128, TOK], F32)
    qcol = sbuf(p4, "qcol", [128, 32], F32)
    rwt_s = sbuf(p4, "rwt_s", [128, 32, 128], F32)
    at_s = sbuf(p4, "at_s", [128, 32, 128], BF16)
    bt_s = sbuf(p4, "bt_s", [128, 32, 128], BF16)
    tsc = sbuf(p4, "tsc", [128, TOK], BF16)
    rrow = sbuf(p4, "rrow", [1, 128], F32)
    rcol = sbuf(p4, "rcol", [128, 1], F32)
    wf4 = [sbuf(p4, f"wf4_{i}", [128, 32, 128], BF16) for i in range(2)]
    ffb = [sbuf(p4, f"ffb{i}", [128, TOK], BF16) for i in range(2)]
    sqf = sbuf(p4, "sqf", [128, TOK], BF16)
    xmr4 = [sbuf(p4, f"xmr4_{i}", [128, TOK], F32) for i in range(2)]
    ffr = [sbuf(p4, f"ffr{i}", [128, TOK], BF16) for i in range(2)]
    outb = [sbuf(p4, f"outb{i}", [128, TOK], F32) for i in range(2)]
    INC("g", lambda g: g.dma_start(out=rwt_s[:, :, :], in_=rWTd.rearrange("(a p) e -> p a e", p=128)), ldg, 16)
    INC("g", lambda g: g.dma_start(out=at_s[:, :, :], in_=ATd.rearrange("(a p) e -> p a e", p=128)), ldg, 16)
    INC("g", lambda g: g.dma_start(out=bt_s[:, :, :], in_=BTd.rearrange("p (a o) -> p a o", a=32)), ldg, 16)
    ldg4 = ldg.v
    W("t", ldg)
    OP("t", lambda t: t.matmul(pf[0][:, 0:512], lhsT=ones1f[:, :], rhs=wmask_s[:, 0:512], start=True, stop=True))
    INC("t", lambda t: t.matmul(pf[1][:, 0:512], lhsT=ones1f[:, :], rhs=wmask_s[:, 512:1024], start=True, stop=True), pe, 1)
    pe_wq = pe.v
    W("v", pe, pe_wq)
    for i in range(32):
        OP("v", lambda v, ii=i: v.scalar_tensor_tensor(
            out=dump[:, 0:512], in0=xln2[:, ii, 0:512], scalar=1.0, in1=pf[0][:, 0:512],
            op0=ALU.mult, op1=ALU.mult))
        OP("v", lambda v, ii=i: v.scalar_tensor_tensor(
            out=dump[:, 512:1024], in0=xln2[:, ii, 512:1024], scalar=1.0, in1=pf[1][:, 0:512],
            op0=ALU.mult, op1=ALU.mult))
        INC("v", lambda v, ii=i: v.tensor_reduce(out=qcol[:, ii:ii + 1], in_=dump[:, :],
                                                 axis=AX.X, op=ALU.add), ve, 1)
    W("v", ve)
    INC("v", lambda v: v.engine_nop(), ve, 1)
    W("y", ve)
    INC("y", lambda y: y.dma_start(out=cq_in[:, :], in_=qcol[:, :]), ldy, 16)
    W("g", ldy)
    INC("g", lambda g: g.collective_compute(
        "AllReduce", ALU.add, replica_groups=[[0, 1, 2, 3], [4, 5, 6, 7]],
        ins=[cq_in_t.ap().opt()], outs=[cq_out_t.ap().opt()]), scc, 1)
    W("y", scc)
    INC("y", lambda y: y.dma_start(out=qcol[:, :], in_=cq_out[:, :]), ldy, 16)
    if debug:
        W("y", ldy)
        INC("y", lambda y: y.dma_start(out=dbg["q"], in_=qcol[:, :]), ldy, 16)
    W("t", ldy); W("t", ldg, ldg4)
    for k in range(32):
        fn = lambda t, kk=k: t.matmul(pf[2][0:1, 0:128], lhsT=qcol[:, kk:kk + 1], rhs=rwt_s[:, kk, :],
                                      start=(kk == 0), stop=(kk == 31))
        if k == 31:
            INC("t", fn, pe, 1)
        else:
            OP("t", fn)
    W("v", pe)
    INC("v", lambda v: v.tensor_copy(out=rrow[:, :], in_=pf[2][0:1, 0:128]), ve, 1)
    rwide = sbuf(p4, "rwide", [128, 128], F32)
    rmax = sbuf(p4, "rmax", [128, 1], F32)
    rsum = sbuf(p4, "rsum", [128, 1], F32)
    W("t", ve, ve.v)
    OP("t", lambda t: t.matmul(pf[3][:, 0:128], lhsT=ones1f[:, :], rhs=rrow[:, :], start=True, stop=False))
    INC("t", lambda t: t.matmul(pf[3][:, 0:128], lhsT=ones1f[:, :], rhs=rb_s[:, :], start=False, stop=True), pe, 1)
    W("v", pe)
    INC("v", lambda v: v.tensor_copy(out=rwide[:, :], in_=pf[3][:, 0:128]), ve, 1)
    rwe4 = sbuf(p4, "rwe4", [128, 128], F32)
    INC("v", lambda v: v.tensor_reduce(out=rmax[:, :], in_=rwide[:, 0:128], axis=AX.X, op=ALU.max), ve, 1)
    W("v", ve)
    INC("v", lambda v: v.tensor_scalar(out=rwide[:, 0:128], in0=rwide[:, 0:128], scalar1=rmax[:, 0:1],
                                       scalar2=None, op0=ALU.subtract), ve, 1)
    W("s", ve)
    INC("s", lambda s: s.activation(out=rwe4[:, :], in_=rwide[:, 0:128], func=AF.Exp), se, 1)
    W("v", se)
    INC("v", lambda v: v.tensor_reduce(out=rsum[:, :], in_=rwe4[:, 0:128], axis=AX.X, op=ALU.add), ve, 1)
    W("v", ve)
    INC("v", lambda v: v.reciprocal(out=rsum[:, :], in_=rsum[:, :]), ve, 1)
    W("v", ve)
    OP("v", lambda v: v.tensor_scalar(out=rwide[:, 0:128], in0=rwe4[:, 0:128], scalar1=rsum[:, 0:1],
                                      scalar2=2.0, op0=ALU.mult, op1=ALU.mult))
    INC("v", lambda v: v.tensor_copy(out=rrow[:, 0:128], in_=rwide[0:1, 0:128]), ve, 1)
    if debug:
        W("y", ve)
        INC("y", lambda y: y.dma_start(out=dbg["rt"], in_=rrow[:, 0:8]), ldy, 16)
        INC("y", lambda y: y.dma_start(out=dbg["rwide"], in_=rwide[:, :]), ldy, 16)
        INC("y", lambda y: y.dma_start(out=dbg["rmx"][:, 0:1], in_=rmax[:, :], allow_slow_non_contiguous=True), ldy, 16)
        INC("y", lambda y: y.dma_start(out=dbg["rmx"][:, 1:2], in_=rsum[:, :], allow_slow_non_contiguous=True), ldy, 16)
        INC("y", lambda y: y.dma_start(out=dbg["lgr"], in_=rrow[:, :]), ldy, 16)
    W("y", ve)
    INC("y", lambda y: y.dma_start(out=rt_d, in_=rrow[:, 0:8]), ldy, 16)
    W("y", ldy)
    for e_ in range(8):
        INC("y", lambda y, ee=e_: y.dma_start(
            out=rcol[ee * 16:(ee + 1) * 16, 0:1],
            in_=bass.AP(rt_d.tensor, ee, [[0, 16], [1, 1]])), ldy, 16)
    W("t", ldy)
    for tb in range(2):
        pk = 3 if tb == 0 else 4
        for k in range(32):
            fn = lambda t, kk=k, p=pk, b=tb: t.matmul(
                pf[p][:, 0:512], lhsT=at_s[:, kk, :], rhs=xln2[:, kk, b * 512:(b + 1) * 512],
                start=(kk == 0), stop=(kk == 31))
            if k == 31 and tb == 1:
                INC("t", fn, pe, 1)
            else:
                OP("t", fn)
    W("v", pe)
    OP("v", lambda v: v.tensor_scalar(out=tsc[:, 0:512], in0=pf[3][:, 0:512], scalar1=rcol[:, 0:1],
                                      scalar2=None, op0=ALU.mult))
    INC("v", lambda v: v.tensor_scalar(out=tsc[:, 512:1024], in0=pf[4][:, 0:512], scalar1=rcol[:, 0:1],
                                       scalar2=None, op0=ALU.mult), ve, 1)
    ve_tsc = ve.v
    WfT3 = WfT.rearrange("(a p) o -> p a o", p=128)
    pe_mm4 = {}; se_ev4 = {}
    for ob in range(32):
        sl = ob % 2
        if ob >= 2:
            W("g", pe, pe_mm4[ob - 2])
        INC("g", lambda g, o=ob, s=sl: g.dma_start(
            out=wf4[s][:, :, :], in_=WfT3[:, :, o * 128:(o + 1) * 128]), wp[sl], 16)
        W("t", wp[sl]); W("t", ve, ve_tsc)
        if ob >= 2:
            W("t", se, se_ev4[ob - 2])
        pA, pB = (0, 1) if sl == 0 else (2, 3)
        for tb in range(2):
            pk = pA if tb == 0 else pB
            for k in range(32):
                OP("t", lambda t, s=sl, kk=k, p=pk, b=tb: t.matmul(
                    pf[p][:, 0:512], lhsT=wf4[s][:, kk, :],
                    rhs=xln2[:, kk, b * 512:(b + 1) * 512],
                    start=(kk == 0), stop=False))
            fn_ = lambda t, o=ob, p=pk, b=tb: t.matmul(
                pf[p][:, 0:512], lhsT=bt_s[:, o, :], rhs=tsc[:, b * 512:(b + 1) * 512],
                start=False, stop=True)
            if tb == 1:
                INC("t", fn_, pe, 1)
            else:
                OP("t", fn_)
        pe_mm4[ob] = pe.v
        W("s", pe, pe_mm4[ob])
        if ob >= 2:
            W("s", sp[sl], sp[sl].v)
        OP("s", lambda s, o=ob, p=pA, z=sl: s.activation(
            out=ffb[z][:, 0:512], in_=pf[p][:, 0:512], func=AF.Identity, bias=ffnb_s[:, o:o + 1]))
        INC("s", lambda s, o=ob, p=pB, z=sl: s.activation(
            out=ffb[z][:, 512:1024], in_=pf[p][:, 0:512], func=AF.Identity, bias=ffnb_s[:, o:o + 1]), se, 1)
        se_ev4[ob] = se.v
        W("t", se, se.v)
        OP("t", lambda t, z=sl, f=(ob == 0), l=(ob == 31): t.matmul(
            pf[4][0:1, 0:512], lhsT=onescb[:, :], rhs=ffb[z][:, 0:512], start=f, stop=l))
        INC("t", lambda t, z=sl, f=(ob == 0), l=(ob == 31): t.matmul(
            pf[5][0:1, 0:512], lhsT=onescb[:, :], rhs=ffb[z][:, 512:1024], start=f, stop=l), pe, 1)
        pe_mm4[ob] = pe.v
        W("y", se, se_ev4[ob])
        INC("y", lambda y, o=ob, z=sl: y.dma_start(out=sp_ffn[o, :, :], in_=ffb[z][:, :]), sp[sl], 16)
    barrier()
    for i in range(32):
        sl = i % 2
        if i >= 2:
            W("y", se, se.v - 1)
        INC("y", lambda y, ii=i, s=sl: y.dma_start(out=ffr[s][:, :], in_=sp_ffn[ii, :, :]), rd[sl], 16)
        W("s", rd[sl])
        INC("s", lambda s, z=sl: s.activation(out=sqf[:, :], in_=ffr[z][:, :], func=AF.Square), se, 1)
        W("t", se, se.v)
        OP("t", lambda t, f=(i == 0), l=(i == 31): t.matmul(
            pf[0][0:1, 0:512], lhsT=onescb[:, :], rhs=sqf[:, 0:512], start=f, stop=l))
        fn = lambda t, f=(i == 0), l=(i == 31): t.matmul(
            pf[1][0:1, 0:512], lhsT=onescb[:, :], rhs=sqf[:, 512:1024], start=f, stop=l)
        if i == 31:
            INC("t", fn, pe, 1)
        else:
            OP("t", fn)
    W("v", pe)
    OP("v", lambda v: v.tensor_copy(out=rowA[:, 0:512], in_=pf[4][0:1, 0:512]))
    OP("v", lambda v: v.tensor_copy(out=rowA[:, 512:1024], in_=pf[5][0:1, 0:512]))
    OP("v", lambda v: v.tensor_copy(out=rowC[:, 0:512], in_=pf[0][0:1, 0:512]))
    OP("v", lambda v: v.tensor_copy(out=rowC[:, 512:1024], in_=pf[1][0:1, 0:512]))
    INC("v", lambda v: v.engine_nop(), ve, 1)
    stats_finalize(HID, 1024)
    if debug:
        W("y", ve)
        INC("y", lambda y: y.dma_start(out=dbg["r4"][0], in_=rowA[:, :]), ldy, 16)
        INC("y", lambda y: y.dma_start(out=dbg["r4"][1], in_=rowB[:, :]), ldy, 16)
    W("t", ve)
    for wi, (row, pA_, pB_) in enumerate([(rowA, 0, 1), (rowB, 2, 3)]):
        OP("t", lambda t, r=row, p=pA_: t.matmul(pf[p][:, 0:512], lhsT=ones1f[:, :], rhs=r[:, 0:512], start=True, stop=True))
        fn_ = lambda t, r=row, p=pB_: t.matmul(pf[p][:, 0:512], lhsT=ones1f[:, :], rhs=r[:, 512:1024], start=True, stop=True)
        if wi == 1:
            INC("t", fn_, pe, 1)
        else:
            OP("t", fn_)
    pe_b5 = pe.v
    for i in range(32):
        sl = i % 2
        if i >= 2:
            W("y", ve, ve.v - 1)
        else:
            W("y", se, se.v)
        INC("y", lambda y, ii=i, s=sl: y.dma_start(out=ffr[s][:, :], in_=sp_ffn[ii, :, :]), rd[sl], 16)
        INC("y", lambda y, ii=i, s=sl: y.dma_start(out=xmr4[s][:, :], in_=sp_xmid[ii, :, :]), rd[sl], 16)
        W("v", rd[sl]); W("v", pe, pe_b5)
        for (c0, pm, pr_) in [(0, 0, 2), (512, 1, 3)]:
            OP("v", lambda v, s=sl, a=c0, p=pm: v.scalar_tensor_tensor(
                out=dump[:, a:a + 512], in0=pf[p][:, 0:512], scalar=-1.0, in1=ffr[s][:, a:a + 512],
                op0=ALU.mult, op1=ALU.add))
            OP("v", lambda v, a=c0, p=pr_: v.tensor_tensor(
                out=dump[:, a:a + 512], in0=dump[:, a:a + 512], in1=pf[p][:, 0:512], op=ALU.mult))
            OP("v", lambda v, s=sl, a=c0, z=sl: v.tensor_tensor(
                out=outb[z][:, a:a + 512], in0=dump[:, a:a + 512], in1=xmr4[s][:, a:a + 512], op=ALU.add))
        INC("v", lambda v: v.engine_nop(), ve, 1)
        W("y", ve, ve.v)
        INC("y", lambda y, ii=i, z=sl: y.dma_start(
            out=outT.rearrange("(a p) t -> p a t", p=128)[:, ii, :], in_=outb[z][:, :]), ldy, 16)
        if debug:
            W("g", ve, ve.v)
            INC("g", lambda g, ii=i, z=sl: g.dma_start(out=dbg["ffn"][ii], in_=ffr[z][:, :]), ldg, 16)
    W("y", ldy)
    barrier()
    p4.close()
    xln2S.close()

    return _finish(nc, ops, root)


def _finish(nc, ops, root):
    with nc.Block() as block:
        @block.tensor
        def _(t):
            for fn in ops["t"]:
                fn(t)

        @block.vector
        def _(v):
            for fn in ops["v"]:
                fn(v)

        @block.scalar
        def _(s):
            for fn in ops["s"]:
                fn(s)

        @block.gpsimd
        def _(g):
            for fn in ops["g"]:
                fn(g)

        @block.sync
        def _(y):
            for fn in ops["y"]:
                fn(y)
    try:
        root.close()
    except AssertionError:
        pass
    return nc


_CACHE = {}


def _layout(inputs):
    f32 = np.float32
    W = {}
    W["WpT"] = np.ascontiguousarray(np.asarray(inputs["L0_attn_in_proj_w"], f32).T)
    W["WoT"] = np.ascontiguousarray(np.asarray(inputs["L0_attn_out_w"], f32).T)
    W["WfT"] = np.ascontiguousarray(np.asarray(inputs["L0_ffn_W"], f32).T)
    A = np.asarray(inputs["L0_ffn_A"], f32)
    W["ATd"] = np.ascontiguousarray(A.transpose(2, 0, 1).reshape(HID, 128))
    Bw = np.asarray(inputs["L0_ffn_B"], f32)
    W["BTd"] = np.ascontiguousarray(Bw.transpose(0, 2, 1).reshape(128, HID))
    cw = np.asarray(inputs["L0_attn_conv_w"], f32)[:, 0, :]
    W["convw"] = np.ascontiguousarray(cw.reshape(80, 128, 4).transpose(1, 0, 2).reshape(128, 320))
    cb = np.asarray(inputs["L0_attn_conv_b"], f32)
    W["convb"] = np.ascontiguousarray(cb.reshape(80, 128).T)
    W["dtbd"] = np.asarray(inputs["L0_attn_dt_bias"], f32).reshape(128, 1)
    W["alogd"] = np.asarray(inputs["L0_attn_A_log"], f32).reshape(128, 1)
    D = np.asarray(inputs["L0_attn_D"], f32)
    dexp = np.empty((128, 64), f32)
    for ct in range(64):
        dexp[0:64, ct] = D[2 * ct]
        dexp[64:128, ct] = D[2 * ct + 1]
    W["dexpd"] = dexp
    nw = np.asarray(inputs["L0_attn_norm_w"], f32)
    W["nwexpd"] = np.ascontiguousarray(nw.reshape(64, 128).T)
    W["ffnbd"] = np.ascontiguousarray(np.asarray(inputs["L0_ffn_b"], f32).reshape(32, 128).T)
    rwt = np.zeros((HID, 128), f32)
    rwt[:, 0:8] = np.asarray(inputs["L0_ffn_router_W"], f32).T
    W["rWTd"] = rwt
    rbp = np.full((1, 128), -1.0e9, f32)
    rbp[0, 0:8] = np.asarray(inputs["L0_ffn_router_b"], f32)
    W["rbd"] = rbp
    wm = np.zeros((1, TOK), f32)
    return W, wm


def kernel(**inputs):
    X = np.asarray(inputs["L0_attn_input_states"], np.float32)
    Wsh, _ = _layout(inputs)
    key = "nc"
    if key not in _CACHE:
        _CACHE[key] = build(debug=DEBUG)
    nc = _CACHE[key]
    in_maps = []
    for core in range(8):
        b, q = core // 4, core % 4
        t0 = q * TOK
        xh = np.zeros((TOK + 3, HID), np.float32)
        xh[3:] = X[b, t0:t0 + TOK]
        if t0 > 0:
            xh[0:3] = X[b, t0 - 3:t0]
        m = dict(Wsh)
        m["xT"] = np.ascontiguousarray(xh.T)
        wm = np.zeros((1, TOK), np.float32)
        lo = max(611 - t0, 0)
        hi = min(4095 - t0, TOK)
        if hi > lo:
            wm[0, lo:hi] = 1.0 / QCOUNT
        m["wmaskd"] = wm
        cm = np.zeros((128, 3), np.float32)
        cm[:, 0:q] = 1.0
        m["cmskd"] = cm
        in_maps.append(m)
    res = run_bass_kernel_spmd(nc, in_maps, core_ids=list(range(8)))
    out = np.empty((2, 4096, HID), np.float32)
    for core in range(8):
        b, q = core // 4, core % 4
        out[b, q * TOK:(q + 1) * TOK, :] = res.results[core]["outT"].T
    return out
